# version 02 (final)
# baseline (speedup 1.0000x reference)
import contextlib
import math

import numpy as np
import ml_dtypes

import concourse.bass as bass
import concourse.mybir as mybir
from concourse.bass_isa import ReduceOp
from concourse.bass_utils import run_bass_kernel_spmd

F32 = mybir.dt.float32
BF16 = mybir.dt.bfloat16
I32 = mybir.dt.int32
AF = mybir.ActivationFunctionType
ALU = mybir.AluOpType
AX = mybir.AxisListType

D = 2048
B = 4
HN = 16
S = 2048
TQ = 1024
TK = 2048
NCH = 16
G = 4
NG = HN // G
II = 8192
EPS = 1e-5
ISQ_DK = float(1.0 / math.sqrt(128))
C_FWHT = float(np.float32(1.0 / np.sqrt(np.float64(II))))
INV_D = 1.0 / 2048.0
INV_WN = 1.0 / float(II * D)

DEBUG = False


def build_nc(debug=False, upto=99):
    import os as _os
    _norace = _os.environ.get("MK_NORACE", "0") == "1"
    nc = bass.Bass("TRN2", target_bir_lowering=False,
                   detect_race_conditions=not _norace)

    xfm = nc.dram_tensor("xfm", [D, S], F32, kind="ExternalInput").ap()
    WIN = {n: nc.dram_tensor(n, shp, F32, kind="ExternalInput").ap()
           for n, shp in (("wq", [D, D]), ("wk", [D, D]), ("wv", [D, D]),
                          ("wo", [D, D]), ("wg", [II, D]), ("wu", [II, D]),
                          ("wd", [D, II]))}
    BIN = {n: nc.dram_tensor(n, [1, D], F32, kind="ExternalInput").ap()
           for n in ("bq", "bk", "bv", "bo")}
    TIN = {n: nc.dram_tensor(n, [128, t], F32, kind="ExternalInput").ap()
           for n, t in (("t1q", TQ), ("t2q", TQ), ("t1k", TK), ("t2k", TK))}
    masksd = nc.dram_tensor("masks", [16 * 128, TQ], BF16, kind="ExternalInput").ap()
    h128d = nc.dram_tensor("h128", [128, 128], BF16, kind="ExternalInput").ap()
    h64bd = nc.dram_tensor("h64b", [128, 128], BF16, kind="ExternalInput").ap()
    identd = nc.dram_tensor("ident", [128, 128], BF16, kind="ExternalInput").ap()
    identfd = nc.dram_tensor("identf", [128, 128], F32, kind="ExternalInput").ap()

    outfm = nc.dram_tensor("outfm", [D, TQ], F32, kind="ExternalOutput").ap()

    WBF = {n: nc.dram_tensor(n + "b", [D, D], BF16, kind="Internal").ap()
           for n in ("wq", "wk", "wv", "wo")}
    tgd = nc.dram_tensor("tgd", [II, D], BF16, kind="Internal").ap()
    tud = nc.dram_tensor("tud", [II, D], BF16, kind="Internal").ap()
    tdd = nc.dram_tensor("tdd", [D, II], BF16, kind="Internal").ap()
    fwbd = nc.dram_tensor("fwbd", [II, TQ], BF16, kind="Internal").ap()
    xnewd = nc.dram_tensor("xnewd", [D, TQ], F32, kind="Internal").ap()
    y2dd = nc.dram_tensor("y2dd", [D, TQ], F32, kind="Internal").ap()

    taps = {}
    if debug:
        for n, shp in (("xln", [128, NCH * S]), ("qg", [128, HN * TQ]),
                       ("kg", [128, HN * TK]), ("aft", [128, HN * TQ]),
                       ("xq", [128, NCH * TQ]), ("fw", [128, 64 * TQ])):
            taps[n] = nc.dram_tensor("dbg_" + n, shp,
                                     BF16, kind="ExternalOutput").ap()
        for n, shp in (("y", [128, NCH * TQ]), ("xnew", [128, NCH * TQ])):
            taps[n] = nc.dram_tensor("dbg_" + n, shp, F32, kind="ExternalOutput").ap()
        taps["vg"] = nc.dram_tensor("dbg_vg", [128, 16 * 4 * 129], BF16, kind="ExternalOutput").ap()
        taps["pt0"] = nc.dram_tensor("dbg_pt0", [128, TQ], BF16, kind="ExternalOutput").ap()
        taps["pt1"] = nc.dram_tensor("dbg_pt1", [128, TQ], BF16, kind="ExternalOutput").ap()
        taps["rd"] = nc.dram_tensor("dbg_rd", [128, 8], F32, kind="ExternalOutput").ap()
        taps["ppv"] = nc.dram_tensor("dbg_ppv", [128, 2048], F32, kind="ExternalOutput").ap()
        taps["atm"] = nc.dram_tensor("dbg_atm", [128, 4 * 8 * 128], BF16, kind="ExternalOutput").ap()
        taps["scols"] = nc.dram_tensor("dbg_scols", [128, 8], F32, kind="ExternalOutput").ap()
        taps["tgd"] = nc.dram_tensor("dbg_tgd", [1024, 2048], BF16, kind="ExternalOutput").ap()
        taps["tud"] = nc.dram_tensor("dbg_tud", [1024, 2048], BF16, kind="ExternalOutput").ap()
        taps["fwba"] = nc.dram_tensor("dbg_fwba", [2048, TQ], BF16, kind="ExternalOutput").ap()
        taps["fwpre"] = nc.dram_tensor("dbg_fwpre", [128, 16 * TQ], BF16, kind="ExternalOutput").ap()
        taps["bdb"] = nc.dram_tensor("dbg_bdb", [128, TQ], F32, kind="ExternalOutput").ap()
        taps["a2b"] = nc.dram_tensor("dbg_a2b", [128, TQ], F32, kind="ExternalOutput").ap()

    cnt = {}

    def inc(instr, sem, n=1):
        instr.then_inc(sem, n)
        cnt[sem.name] = cnt.get(sem.name, 0) + n
        return cnt[sem.name]

    def cur(sem):
        return cnt.get(sem.name, 0)

    es = contextlib.ExitStack()
    SEM = {n: es.enter_context(nc.semaphore(n))
           for n in ("dsy", "dgp", "mm", "vv", "ss", "gg", "bar",
                     "dsA", "dsB", "dgA", "dgB")}
    dsy, dgp = SEM["dsy"], SEM["dgp"]
    mm, vv, ss, gg, bar = SEM["mm"], SEM["vv"], SEM["ss"], SEM["gg"], SEM["bar"]
    dsP = (SEM["dsA"], SEM["dsB"])
    dgP = (SEM["dgA"], SEM["dgB"])

    sb = lambda *a, **k: es.enter_context(nc.sbuf_tensor(*a, **k))

    s_ones_r = sb("s_ones_r", [1, 512], BF16)
    s_zero_r = sb("s_zero_r", [1, 128], BF16)
    s_ones_c = sb("s_ones_c", [128, 1], BF16)
    s_ones_rf = sb("s_ones_rf", [1, 128], F32)
    s_ones_cf = sb("s_ones_cf", [128, 1], F32)
    s_o128 = sb("s_o128", [128, 128], BF16)
    s_o128f = sb("s_o128f", [128, 128], F32)
    s_h128 = sb("s_h128", [128, 128], BF16)
    s_h64b = sb("s_h64b", [128, 128], BF16)
    s_ident = sb("s_ident", [128, 128], BF16)
    s_identf = sb("s_identf", [128, 128], F32)
    s_t = {n: sb("s_" + n, [128, TQ if "q" in n else TK], BF16)
           for n in ("t1q", "t2q", "t1k", "t2k")}
    s_cols = sb("s_cols", [128, 8], F32)
    s_icol = sb("s_icol", [128, 4], F32)

    nbar = [0]
    _open_stacks = []

    def _mkstack():
        st = contextlib.ExitStack()
        _open_stacks.append(st)
        return st

    def _close(st):
        _open_stacks.remove(st)
        st.close()

    def _abort_close():
        for st in reversed(_open_stacks):
            st.close()
        _open_stacks.clear()

    def barrier(blk):
        nbar[0] += 1
        t = 5 * nbar[0]
        @blk.sync
        def _(sync):
            sync.wait_ge(dsy, cur(dsy))
            sync.wait_ge(dsP[0], cur(dsP[0]))
            sync.wait_ge(dsP[1], cur(dsP[1]))
            sync.sem_inc(bar, 1)
            sync.wait_ge(bar, t)
        @blk.gpsimd
        def _(gpsimd):
            gpsimd.wait_ge(dgp, cur(dgp))
            gpsimd.wait_ge(dgP[0], cur(dgP[0]))
            gpsimd.wait_ge(dgP[1], cur(dgP[1]))
            gpsimd.sem_inc(bar, 1)
            gpsimd.wait_ge(bar, t)
        @blk.scalar
        def _(scalar):
            scalar.sem_inc(bar, 1)
            scalar.wait_ge(bar, t)
        @blk.tensor
        def _(tensor):
            tensor.sem_inc(bar, 1)
            tensor.wait_ge(bar, t)
        @blk.vector
        def _(vector):
            vector.sem_inc(bar, 1)
            vector.wait_ge(bar, t)


    def emit_ln_stats(blk, pS, src_slices, nchunks, xb, xsqb, width):
        b_ss, b_vv = cur(ss), cur(vv)
        b_mm = cur(mm)

        @blk.scalar
        def _(scalar):
            for c in range(nchunks):
                if c >= 1:
                    scalar.wait_ge(mm, b_mm + c)
                inc(scalar.activation(xsqb[:], src_slices(c), AF.Square), ss)

        @blk.vector
        def _(vector):
            for c in range(nchunks):
                if c >= 1:
                    vector.wait_ge(mm, b_mm + c)
                inc(vector.tensor_copy(xb[:], src_slices(c)), vv)

        @blk.tensor
        def _(tensor):
            for c in range(nchunks):
                tensor.wait_ge(ss, b_ss + c + 1)
                tensor.wait_ge(vv, b_vv + c + 1)
                for j in range(width // 512):
                    sp = slice(j * 512, (j + 1) * 512)
                    tensor.matmul(pS[:, sp], s_o128[:], xb[0:128, sp],
                                  start=(c == 0), stop=(c == nchunks - 1))
                    r = tensor.matmul(pS[:, width + j * 512:width + (j + 1) * 512],
                                      s_o128[:], xsqb[0:128, sp],
                                      start=(c == 0), stop=(c == nchunks - 1))
                inc(r, mm)
        return cur(mm)

    def emit_ln_rows_bcast(blk, pS, width, mbt, rbt, rows=None):
        b0 = cur(mm)

        @blk.vector
        def _(vector):
            vector.wait_ge(mm, b0)
            vector.tensor_scalar(mbt[:], pS[:, 0:width], INV_D, None, ALU.mult)
            vector.tensor_scalar(rbt[:], pS[:, width:2 * width], INV_D, None, ALU.mult)
            vector.tensor_tensor(pS[:, 0:width], mbt[:], mbt[:], ALU.mult)
            vector.tensor_tensor(rbt[:], rbt[:], pS[:, 0:width], ALU.subtract)
            inc(vector.tensor_scalar(rbt[:], rbt[:], EPS, None, ALU.add), vv)

        b1 = cur(vv)

        @blk.scalar
        def _(scalar):
            scalar.wait_ge(vv, b1)
            scalar.activation(rbt[:], rbt[:], AF.Ln)
            inc(scalar.activation(rbt[:], rbt[:], AF.Exp, scale=-0.5), ss)

        @blk.vector
        def _(vector):
            vector.wait_ge(ss, cur(ss))
            inc(vector.nop(), vv)
        return cur(vv)

    def emit_round_clip_stream(blk, n_tiles, z_emit, out_emit, zt, si, tf, c2,
                               lo, hi, vz_incs):
        b_ss, b_vv = cur(ss), cur(vv)
        per = vz_incs + 1

        @blk.scalar
        def _(scalar):
            for i in range(n_tiles):
                scalar.wait_ge(vv, b_vv + per * i + vz_incs)
                scalar.copy(si[:], zt[:])
                inc(scalar.copy(tf[:], si[:]), ss)

        @blk.vector
        def _(vector):
            for i in range(n_tiles):
                z_emit(vector, i)
                vector.wait_ge(ss, b_ss + i + 1)
                vector.scalar_tensor_tensor(c2[:], zt[:], -0.5, tf[:], ALU.add, ALU.is_gt)
                vector.tensor_tensor(tf[:], tf[:], c2[:], ALU.add)
                vector.scalar_tensor_tensor(c2[:], zt[:], 0.5, tf[:], ALU.add, ALU.is_lt)
                vector.tensor_tensor(tf[:], tf[:], c2[:], ALU.subtract)
                vector.tensor_scalar(tf[:], tf[:], float(hi), float(lo), ALU.min, ALU.max)
                out_emit(vector, i)
        return cur(vv)


    def emit_pmax_bcast(blk, src, outb, ps, cols8, rowbuf):
        b0_vv, b0_ss = cur(vv), cur(ss)

        for w in range(2):
            bt_vv = cur(vv)

            @blk.tensor
            def _(tensor, w=w, bt_vv=bt_vv):
                tensor.wait_ge(vv, bt_vv)
                if w == 0:
                    tensor.wait_ge(ss, b0_ss)
                for i in range(4):
                    b = w * 4 + i
                    inc(tensor.transpose(ps[:, i * 512:i * 512 + 128],
                                         src[:, b * 128:(b + 1) * 128], s_identf[:]), mm)

            bt_mm = cur(mm)

            @blk.vector
            def _(vector, w=w, bt_mm=bt_mm):
                vector.wait_ge(mm, bt_mm)
                r = None
                for i in range(4):
                    b = w * 4 + i
                    r = vector.tensor_reduce(cols8[:, b:b + 1], ps[:, i * 512:i * 512 + 128],
                                             AX.X, ALU.max)
                inc(r, vv)

        b2_vv = cur(vv)

        @blk.tensor
        def _(tensor):
            tensor.wait_ge(vv, b2_vv)
            for b in range(8):
                inc(tensor.transpose(ps[0:1, 1024 + b * 128:1024 + (b + 1) * 128],
                                     cols8[:, b:b + 1], s_identf[:]), mm)

        b3_mm = cur(mm)

        @blk.vector
        def _(vector):
            vector.wait_ge(mm, b3_mm)
            inc(vector.tensor_copy(rowbuf[0:1, :], ps[0:1, 1024:2048]), vv)

        b4_vv = cur(vv)

        @blk.tensor
        def _(tensor):
            tensor.wait_ge(vv, b4_vv)
            r = None
            for j in range(2):
                r = tensor.matmul(ps[:, j * 512:(j + 1) * 512], s_o128f[:],
                                  rowbuf[:, j * 512:(j + 1) * 512], start=True, stop=True)
            inc(r, mm)

        b5_mm = cur(mm)

        @blk.vector
        def _(vector):
            vector.wait_ge(mm, b5_mm)
            inc(vector.tensor_copy(outb[:], ps[:, 0:1024]), vv)

    with es, nc.Block() as blk:
        @blk.sync
        def _(sync):
            inc(sync.dma_start(out=s_h128[:], in_=h128d[:]), dsy, 16)
            inc(sync.dma_start(out=s_h64b[:], in_=h64bd[:]), dsy, 16)
            inc(sync.dma_start(out=s_ident[:], in_=identd[:]), dsy, 16)
            inc(sync.dma_start(out=s_identf[:], in_=identfd[:]), dsy, 16)

        @blk.gpsimd
        def _(gpsimd):
            for n in ("t1q", "t2q", "t1k", "t2k"):
                inc(gpsimd.dma_start(out=s_t[n][:], in_=TIN[n][:]), dgp, 16)
            gpsimd.wait_ge(dgp, cur(dgp))
            for n in ("wq", "wk", "wv", "wo"):
                for j in range(4):
                    inc(gpsimd.dma_start(out=WBF[n][j * 512:(j + 1) * 512, :],
                                         in_=WIN[n][j * 512:(j + 1) * 512, :]), dgp, 16)

        cast_gp = cur(dgp)

        @blk.vector
        def _(vector):
            vector.memset(s_ones_r[:], 1.0)
            vector.memset(s_zero_r[:], 0.0)
            vector.memset(s_ones_c[:], 1.0)
            vector.memset(s_ones_rf[:], 1.0)
            vector.memset(s_ones_cf[:], 1.0)
            vector.memset(s_o128[:], 1.0)
            inc(vector.memset(s_o128f[:], 1.0), vv)

        barrier(blk)

        if upto < 1:
            _abort_close()
            return nc
        st_xln = _mkstack()
        xln = st_xln.enter_context(nc.sbuf_tensor("xln", [128, NCH * S], BF16))
        with contextlib.ExitStack() as P:
            pb = lambda *a, **k: P.enter_context(nc.sbuf_tensor(*a, **k))
            pp = lambda *a, **k: P.enter_context(nc.psum_tensor(*a, **k))
            xc = [pb(f"p1x{i}", [128, S], F32) for i in range(2)]
            xb = pb("p1xb", [128, S], BF16)
            xsqb = pb("p1xsq", [128, S], BF16)
            mbt = pb("p1mb", [128, S], F32)
            rbt = pb("p1rb", [128, S], F32)
            pS = pp("p1ps", [128, 2 * S], F32)

            b_ds = (cur(dsP[0]), cur(dsP[1]))
            b_mm = cur(mm)

            @blk.sync
            def _(sync):
                for c in range(NCH):
                    if c >= 2:
                        sync.wait_ge(mm, b_mm + (c - 1))
                    inc(sync.dma_start(out=xc[c % 2][:],
                                       in_=xfm[c * 128:(c + 1) * 128, :]), dsP[c % 2], 16)

            b_ss, b_vv = cur(ss), cur(vv)

            @blk.scalar
            def _(scalar):
                for c in range(NCH):
                    scalar.wait_ge(dsP[c % 2], b_ds[c % 2] + 16 * (c // 2 + 1))
                    if c >= 1:
                        scalar.wait_ge(mm, b_mm + c)
                    inc(scalar.activation(xsqb[:], xc[c % 2][:], AF.Square), ss)

            @blk.vector
            def _(vector):
                for c in range(NCH):
                    vector.wait_ge(dsP[c % 2], b_ds[c % 2] + 16 * (c // 2 + 1))
                    if c >= 1:
                        vector.wait_ge(mm, b_mm + c)
                    inc(vector.tensor_copy(xb[:], xc[c % 2][:]), vv)

            @blk.tensor
            def _(tensor):
                for c in range(NCH):
                    tensor.wait_ge(ss, b_ss + c + 1)
                    tensor.wait_ge(vv, b_vv + c + 1)
                    for j in range(4):
                        sp = slice(j * 512, (j + 1) * 512)
                        tensor.matmul(pS[:, sp], s_o128[:], xb[0:128, sp],
                                      start=(c == 0), stop=(c == NCH - 1))
                        r = tensor.matmul(pS[:, S + j * 512:S + (j + 1) * 512],
                                          s_o128[:], xsqb[0:128, sp],
                                          start=(c == 0), stop=(c == NCH - 1))
                    inc(r, mm)

            rows1 = pb("p1rows", [1, 3 * S], F32)
            emit_ln_rows_bcast(blk, pS, S, mbt, rbt, rows1)

            b2_vv = cur(vv)
            b2_ds = (cur(dsP[0]), cur(dsP[1]))

            @blk.sync
            def _(sync):
                sync.wait_ge(vv, b2_vv)
                for c in range(NCH):
                    if c >= 2:
                        sync.wait_ge(vv, b2_vv + 2 * (c - 1))
                    inc(sync.dma_start(out=xc[c % 2][:],
                                       in_=xfm[c * 128:(c + 1) * 128, :]), dsP[c % 2], 16)

            @blk.vector
            def _(vector):
                for c in range(NCH):
                    vector.wait_ge(dsP[c % 2], b2_ds[c % 2] + 16 * (c // 2 + 1))
                    inc(vector.tensor_tensor(xc[c % 2][:], xc[c % 2][:], mbt[:], ALU.subtract), vv)
                    inc(vector.tensor_tensor(xln[:, c * S:(c + 1) * S],
                                             xc[c % 2][:], rbt[:], ALU.mult), vv)

            if debug:
                dbg_vv = cur(vv)

                @blk.gpsimd
                def _(gpsimd):
                    gpsimd.wait_ge(vv, dbg_vv)
                    for j in range(8):
                        inc(gpsimd.dma_start(out=taps["xln"][:, j * 4096:(j + 1) * 4096],
                                             in_=xln[:, j * 4096:(j + 1) * 4096]), dgp, 16)

            barrier(blk)

        if upto < 2:
            _abort_close()
            return nc
        st_att = _mkstack()
        Aft = st_att.enter_context(nc.sbuf_tensor("Aft", [128, HN * TQ], BF16, side="right"))

        def rope_evac(vector, po, dst, T1, T2, tsl, sc):
            vector.tensor_tensor(sc[0][:], po[0:64, :], T1[0:64, tsl], ALU.mult)
            vector.tensor_tensor(sc[1][:], po[64:128, :], T2[0:64, tsl], ALU.mult)
            vector.tensor_tensor(dst[0:64, :], sc[0][:], sc[1][:], ALU.add)
            vector.tensor_tensor(sc[0][:], po[0:64, :], T2[64:128, tsl], ALU.mult)
            vector.tensor_tensor(sc[1][:], po[64:128, :], T1[64:128, tsl], ALU.mult)
            return vector.tensor_tensor(dst[64:128, :], sc[0][:], sc[1][:], ALU.add)

        for g in range(NG):
            e0 = g * 512
            with contextlib.ExitStack() as P:
                pb = lambda *a, **k: P.enter_context(nc.sbuf_tensor(*a, **k))
                Qg = pb(f"Qg{g}", [128, G * TQ], BF16)
                Kg = pb(f"Kg{g}", [128, G * TK], BF16)
                Vg = pb(f"Vg{g}", [128, 16 * G * 129], BF16)
                wbuf = pb(f"wbuf{g}", [128, NCH * 512], BF16)
                atm = pb(f"atm{g}", [128, G * 8 * 128], BF16)
                rsc = [pb(f"rsc{g}_{i}", [64, 512], F32) for i in range(2)]
                pt = [pb(f"pt{g}_{i}", [128, TQ], BF16) for i in range(2)]
                mbuf = [pb(f"mbuf{g}_{i}", [128, TQ], BF16) for i in range(2)]
                rd = pb(f"rd{g}", [128, 8], F32)
                sbq = pb(f"sbq{g}", [1, 512], BF16)
                sbk = pb(f"sbk{g}", [1, 512], BF16)
                sbv = pb(f"sbv{g}", [1, 512], BF16)

                @blk.gpsimd
                def _(gpsimd):
                    inc(gpsimd.dma_start(out=sbq[:], in_=BIN["bq"][0:1, e0:e0 + 512]), dgp, 16)
                    inc(gpsimd.dma_start(out=sbk[:], in_=BIN["bk"][0:1, e0:e0 + 512]), dgp, 16)
                    inc(gpsimd.dma_start(out=sbv[:], in_=BIN["bv"][0:1, e0:e0 + 512]), dgp, 16)

                bias_gp = cur(dgp)

                with nc.psum_tensor(f"p2_{g}", [128, 2048], F32) as p2:
                    q_dsy = cur(dsy)

                    @blk.sync
                    def _(sync, g=g):
                        if g == 0:
                            sync.wait_ge(dgp, cast_gp)
                        for dc in range(NCH):
                            inc(sync.dma_start(out=wbuf[:, dc * 512:(dc + 1) * 512],
                                               in_=WBF["wq"][e0:e0 + 512, dc * 128:(dc + 1) * 128],
                                               transpose=True), dsy, 16)

                    q_mm, q_vv = cur(mm), cur(vv)

                    @blk.tensor
                    def _(tensor):
                        tensor.wait_ge(dsy, q_dsy + 16 * NCH)
                        tensor.wait_ge(dgp, bias_gp)
                        for t in range(8):
                            ei, sp2 = divmod(t, 2)
                            if t >= 4:
                                tensor.wait_ge(vv, q_vv + (t - 3))
                            po = p2[:, (t % 4) * 512:(t % 4 + 1) * 512]
                            for dc in range(NCH):
                                tensor.matmul(po,
                                              wbuf[:, dc * 512 + ei * 128:dc * 512 + ei * 128 + 128],
                                              xln[:, dc * S + sp2 * 512:dc * S + (sp2 + 1) * 512],
                                              start=(dc == 0), stop=False)
                            inc(tensor.matmul(po, sbq[0:1, ei * 128:ei * 128 + 128],
                                              s_ones_r[0:1, 0:512], start=False, stop=True), mm)

                    @blk.vector
                    def _(vector):
                        for t in range(8):
                            ei, sp2 = divmod(t, 2)
                            vector.wait_ge(mm, q_mm + t + 1)
                            po = p2[:, (t % 4) * 512:(t % 4 + 1) * 512]
                            tsl = slice(sp2 * 512, (sp2 + 1) * 512)
                            dst = Qg[:, ei * TQ + sp2 * 512:ei * TQ + (sp2 + 1) * 512]
                            inc(rope_evac(vector, po, dst, s_t["t1q"], s_t["t2q"], tsl, rsc), vv)

                    k_pre_vv = cur(vv)
                    k_dsy = cur(dsy)

                    @blk.sync
                    def _(sync):
                        sync.wait_ge(vv, k_pre_vv)
                        for dc in range(NCH):
                            inc(sync.dma_start(out=wbuf[:, dc * 512:(dc + 1) * 512],
                                               in_=WBF["wk"][e0:e0 + 512, dc * 128:(dc + 1) * 128],
                                               transpose=True), dsy, 16)

                    k_mm, k_vv = cur(mm), cur(vv)

                    @blk.tensor
                    def _(tensor):
                        tensor.wait_ge(dsy, k_dsy + 16 * NCH)
                        for t in range(16):
                            ei, sp4 = divmod(t, 4)
                            if t >= 4:
                                tensor.wait_ge(vv, k_vv + (t - 3))
                            po = p2[:, (t % 4) * 512:(t % 4 + 1) * 512]
                            for dc in range(NCH):
                                tensor.matmul(po,
                                              wbuf[:, dc * 512 + ei * 128:dc * 512 + ei * 128 + 128],
                                              xln[:, dc * S + sp4 * 512:dc * S + (sp4 + 1) * 512],
                                              start=(dc == 0), stop=False)
                            inc(tensor.matmul(po, sbk[0:1, ei * 128:ei * 128 + 128],
                                              s_ones_r[0:1, 0:512], start=False, stop=True), mm)

                    @blk.vector
                    def _(vector):
                        for t in range(16):
                            ei, sp4 = divmod(t, 4)
                            vector.wait_ge(mm, k_mm + t + 1)
                            po = p2[:, (t % 4) * 512:(t % 4 + 1) * 512]
                            tsl = slice(sp4 * 512, (sp4 + 1) * 512)
                            dst = Kg[:, ei * TK + sp4 * 512:ei * TK + (sp4 + 1) * 512]
                            inc(rope_evac(vector, po, dst, s_t["t1k"], s_t["t2k"], tsl, rsc), vv)

                    v_pre_vv = cur(vv)

                    @blk.vector
                    def _(vector):
                        vector.wait_ge(vv, v_pre_vv)
                        inc(vector.memset(Vg[:], 1.0), vv)

                    v_dsy = cur(dsy)

                    @blk.sync
                    def _(sync):
                        sync.wait_ge(vv, v_pre_vv)
                        for dc in range(NCH):
                            inc(sync.dma_start(out=wbuf[:, dc * 512:(dc + 1) * 512],
                                               in_=WBF["wv"][e0:e0 + 512, dc * 128:(dc + 1) * 128],
                                               transpose=True), dsy, 16)

                    v_mm, v_ss, v_vv = cur(mm), cur(ss), cur(vv)

                    @blk.tensor
                    def _(tensor):
                        tensor.wait_ge(dsy, v_dsy + 16 * NCH)
                        tensor.wait_ge(vv, v_vv)
                        for tt in range(16):
                            if tt >= 4:
                                tensor.wait_ge(ss, v_ss + G * (tt - 3))
                            po = p2[:, (tt % 4) * 512:(tt % 4 + 1) * 512]
                            for dc in range(NCH):
                                tensor.matmul(po, xln[:, dc * S + tt * 128:dc * S + (tt + 1) * 128],
                                              wbuf[:, dc * 512:(dc + 1) * 512],
                                              start=(dc == 0), stop=False)
                            inc(tensor.matmul(po, s_ones_r[0:1, 0:128],
                                              sbv[0:1, 0:512],
                                              start=False, stop=True), mm)

                    @blk.scalar
                    def _(scalar):
                        for tt in range(16):
                            scalar.wait_ge(mm, v_mm + tt + 1)
                            po = p2[:, (tt % 4) * 512:(tt % 4 + 1) * 512]
                            base = tt * G * 129
                            for hh in range(G):
                                inc(scalar.copy(Vg[:, base + hh * 129 + 1:base + hh * 129 + 129],
                                                po[:, hh * 128:(hh + 1) * 128]), ss)

                if debug:
                    dq_vv = cur(vv)
                    dq_ss = cur(ss)

                    @blk.gpsimd
                    def _(gpsimd, g=g):
                        gpsimd.wait_ge(vv, dq_vv)
                        gpsimd.wait_ge(ss, dq_ss)
                        inc(gpsimd.dma_start(out=taps["qg"][:, g * G * TQ:(g + 1) * G * TQ],
                                             in_=Qg[:]), dgp, 16)
                        inc(gpsimd.dma_start(out=taps["kg"][:, g * G * TK:(g + 1) * G * TK],
                                             in_=Kg[:]), dgp, 16)
                        if g == 0:
                            inc(gpsimd.dma_start(out=taps["vg"][:], in_=Vg[:]), dgp, 16)

                with nc.psum_tensor(f"p3s_{g}", [128, 2048], F32) as pS3, \
                     nc.psum_tensor(f"p3pv_{g}", [128, 2048], F32) as pPV:
                    for hh in range(G):
                        if _os.environ.get("MK_HBAR", "0") == "1":
                            barrier(blk)
                        m0, s0, v0 = cur(mm), cur(ss), cur(vv)
                        gp0 = (cur(dgP[0]), cur(dgP[1]))

                        @blk.gpsimd
                        def _(gpsimd, v0=v0, gp0=gp0):
                            for kc in range(16):
                                gpsimd.wait_ge(vv, max(0, v0 + kc - 1 if kc >= 2 else v0 + kc - 2))
                                inc(gpsimd.dma_start(out=mbuf[kc % 2][:],
                                                     in_=masksd[kc * 128:(kc + 1) * 128, :]),
                                    dgP[kc % 2], 16)
                        Vb = lambda kc: kc * G * 129 + hh * 129

                        @blk.tensor
                        def _(tensor, hh=hh, m0=m0, s0=s0, v0=v0, Vb=Vb):
                            tensor.wait_ge(ss, s0)
                            for bk2 in range(4):
                                tensor.matmul(pPV[:, bk2 * 512:(bk2 + 1) * 512],
                                              s_zero_r[0:1, 0:128], s_ones_r[0:1, 0:512],
                                              start=True, stop=False)
                            for kc in range(16):
                                if kc >= 2:
                                    tensor.wait_ge(ss, s0 + kc - 1)
                                slot = kc % 2
                                for sp2 in range(2):
                                    po = pS3[:, slot * 1024 + sp2 * 512:slot * 1024 + (sp2 + 1) * 512]
                                    inc(tensor.matmul(po,
                                                      Kg[:, hh * TK + kc * 128:hh * TK + (kc + 1) * 128],
                                                      Qg[:, hh * TQ + sp2 * 512:hh * TQ + (sp2 + 1) * 512],
                                                      start=True, stop=True), mm)
                                if kc >= 1:
                                    j = kc - 1
                                    tensor.wait_ge(vv, v0 + j + 1)
                                    for qt in range(8):
                                        r = tensor.matmul(pPV[:, qt * 256:qt * 256 + 129],
                                                          pt[j % 2][:, qt * 128:(qt + 1) * 128],
                                                          Vg[:, Vb(j):Vb(j) + 129],
                                                          start=False, stop=False)
                                    inc(r, mm)
                            tensor.wait_ge(vv, v0 + 16)
                            for qt in range(8):
                                r = tensor.matmul(pPV[:, qt * 256:qt * 256 + 129],
                                                  pt[1][:, qt * 128:(qt + 1) * 128],
                                                  Vg[:, Vb(15):Vb(15) + 129],
                                                  start=False, stop=(qt % 2 == 1))
                            inc(r, mm)

                        @blk.scalar
                        def _(scalar, m0=m0, s0=s0):
                            for kc in range(16):
                                scalar.wait_ge(mm, m0 + 3 * kc + 2)
                                slot = kc % 2
                                inc(scalar.activation(pt[slot][:],
                                                      pS3[:, slot * 1024:(slot + 1) * 1024],
                                                      AF.Exp, scale=ISQ_DK), ss)

                        @blk.vector
                        def _(vector, s0=s0, gp0=gp0):
                            for kc in range(16):
                                vector.wait_ge(ss, s0 + kc + 1)
                                vector.wait_ge(dgP[kc % 2], gp0[kc % 2] + 16 * (kc // 2 + 1))
                                slot = kc % 2
                                inc(vector.tensor_tensor(pt[slot][:], pt[slot][:],
                                                         mbuf[kc % 2][:],
                                                         ALU.mult), vv)

                        mpv = cur(mm)

                        @blk.vector
                        def _(vector, mpv=mpv, s0=s0):
                            vector.wait_ge(mm, mpv)
                            vector.wait_ge(ss, s0)
                            r = None
                            for qt in range(8):
                                r = vector.reciprocal(rd[:, qt:qt + 1],
                                                      pPV[:, qt * 256:qt * 256 + 1])
                            inc(r, vv)

                        nrm_vv = cur(vv)

                        @blk.scalar
                        def _(scalar, hh=hh, nrm_vv=nrm_vv):
                            scalar.wait_ge(vv, nrm_vv)
                            r = None
                            for qt in range(8):
                                r = scalar.activation(
                                    atm[:, (hh * 8 + qt) * 128:(hh * 8 + qt + 1) * 128],
                                    pPV[:, qt * 256 + 1:qt * 256 + 129],
                                    AF.Copy, scale=rd[:, qt:qt + 1])
                            inc(r, ss)


                with nc.psum_tensor(f"p3t_{g}", [128, 2048], BF16) as pT:
                    t_ss, t_mm, t_vv = cur(ss), cur(mm), cur(vv)

                    @blk.tensor
                    def _(tensor):
                        tensor.wait_ge(ss, t_ss)
                        for t in range(G * 8):
                            if t >= 2:
                                tensor.wait_ge(vv, t_vv + (t - 1))
                            inc(tensor.transpose(pT[:, (t % 2) * 1024:(t % 2) * 1024 + 128],
                                                 atm[:, t * 128:(t + 1) * 128],
                                                 s_ident[:]), mm)

                    @blk.vector
                    def _(vector):
                        for t in range(G * 8):
                            vector.wait_ge(mm, t_mm + t + 1)
                            hh, qt = divmod(t, 8)
                            hg = g * G + hh
                            inc(vector.tensor_copy(
                                Aft[:, hg * TQ + qt * 128:hg * TQ + (qt + 1) * 128],
                                pT[:, (t % 2) * 1024:(t % 2) * 1024 + 128]), vv)

                if debug and g == 0:
                    da_vv = cur(vv)
                    da_ss = cur(ss)

                    @blk.gpsimd
                    def _(gpsimd):
                        gpsimd.wait_ge(vv, da_vv)
                        gpsimd.wait_ge(ss, da_ss)
                        inc(gpsimd.dma_start(out=taps["atm"][:], in_=atm[:]), dgp, 16)
                        inc(gpsimd.dma_start(out=taps["pt0"][:], in_=pt[0][:]), dgp, 16)
                        inc(gpsimd.dma_start(out=taps["pt1"][:], in_=pt[1][:]), dgp, 16)
                        inc(gpsimd.dma_start(out=taps["rd"][:], in_=rd[:]), dgp, 16)

                if debug and g == NG - 1:
                    d_vv = cur(vv)

                    @blk.gpsimd
                    def _(gpsimd):
                        gpsimd.wait_ge(vv, d_vv)
                        for j in range(4):
                            inc(gpsimd.dma_start(out=taps["aft"][:, j * 4096:(j + 1) * 4096],
                                                 in_=Aft[:, j * 4096:(j + 1) * 4096]), dgp, 16)

                barrier(blk)

        _close(st_xln)

        if upto < 3:
            _abort_close()
            return nc
        st_y = _mkstack()
        y = st_y.enter_context(nc.sbuf_tensor("p4y", [128, NCH * TQ], F32, side="right"))
        mbt = st_y.enter_context(nc.sbuf_tensor("p4mb", [128, TQ], F32, side="right"))
        rbt = st_y.enter_context(nc.sbuf_tensor("p4rb", [128, TQ], F32, side="right"))
        st_p4 = _mkstack()
        with contextlib.ExitStack() as P:
            pb = lambda *a, **k: P.enter_context(nc.sbuf_tensor(*a, **k))
            pp = lambda *a, **k: P.enter_context(nc.psum_tensor(*a, **k))
            ybf = pb("p4ybf", [128, TQ], BF16)
            ysqb = pb("p4ysq", [128, TQ], BF16)
            sbo = pb("p4sbo", [1, 512], BF16)
            wbuf = pb("p4wbuf", [128, NCH * 512], BF16)
            pO = pp("p4po", [128, 1024], F32)
            pS4 = pp("p4ps", [128, 2048], F32)

            sp_done = []
            for sp in range(4):
                o_dsy = cur(dsy)

                @blk.gpsimd
                def _(gpsimd, sp=sp):
                    if sp >= 1:
                        gpsimd.wait_ge(mm, sp_done[sp - 1])
                    inc(gpsimd.dma_start(out=sbo[:], in_=BIN["bo"][0:1, sp * 512:(sp + 1) * 512]),
                        dgp, 16)

                cur_bo_gp = [cur(dgp)]

                @blk.sync
                def _(sync, sp=sp):
                    if sp >= 1:
                        sync.wait_ge(mm, sp_done[sp - 1])
                    for dc in range(NCH):
                        inc(sync.dma_start(out=wbuf[:, dc * 512:(dc + 1) * 512],
                                           in_=WBF["wo"][sp * 512:(sp + 1) * 512,
                                                         dc * 128:(dc + 1) * 128],
                                           transpose=True), dsy, 16)

                pre_ss, pre_vv = cur(ss), cur(vv)

                def emit_stats(tensor, e):
                    r = None
                    for j in range(2):
                        spq = slice(j * 512, (j + 1) * 512)
                        tensor.matmul(pS4[:, spq], s_o128[:], ybf[0:128, spq],
                                      start=(e == 0), stop=(e == 15))
                        r = tensor.matmul(pS4[:, 1024 + j * 512:1024 + (j + 1) * 512],
                                          s_o128[:], ysqb[0:128, spq],
                                          start=(e == 0), stop=(e == 15))
                    inc(r, mm)

                @blk.tensor
                def _(tensor, sp=sp, o_dsy=o_dsy, pre_ss=pre_ss, pre_vv=pre_vv):
                    tensor.wait_ge(dsy, o_dsy + 16 * NCH)
                    tensor.wait_ge(dgp, cur_bo_gp[0])
                    if sp >= 1:
                        tensor.wait_ge(ss, pre_ss)
                        tensor.wait_ge(vv, pre_vv)
                        emit_stats(tensor, sp * 4 - 1)

                o_mm, o_ss, o_vv = cur(mm), cur(ss), cur(vv)

                @blk.tensor
                def _(tensor, sp=sp, o_mm=o_mm, o_ss=o_ss, o_vv=o_vv):
                    for ec in range(4):
                        e = sp * 4 + ec
                        if ec >= 1:
                            tensor.wait_ge(ss, o_ss + 2 * ec)
                            tensor.wait_ge(vv, o_vv + ec)
                        for dc in range(NCH):
                            for j in range(2):
                                tensor.matmul(pO[:, j * 512:(j + 1) * 512],
                                              wbuf[:, dc * 512 + ec * 128:dc * 512 + ec * 128 + 128],
                                              Aft[:, dc * TQ + j * 512:dc * TQ + (j + 1) * 512],
                                              start=(dc == 0), stop=False)
                        for j in range(2):
                            r = tensor.matmul(pO[:, j * 512:(j + 1) * 512],
                                              sbo[0:1, ec * 128:ec * 128 + 128],
                                              s_ones_r[0:1, 0:512], start=False, stop=True)
                        inc(r, mm)
                        if ec >= 1:
                            emit_stats(tensor, sp * 4 + ec - 1)
                    if sp == 3:
                        tensor.wait_ge(ss, o_ss + 8)
                        tensor.wait_ge(vv, o_vv + 4)
                        emit_stats(tensor, 15)

                @blk.scalar
                def _(scalar, sp=sp, o_mm=o_mm):
                    for ec in range(4):
                        e = sp * 4 + ec
                        scalar.wait_ge(mm, o_mm + (1 if ec == 0 else 2 * ec + 1))
                        inc(scalar.copy(y[:, e * TQ:(e + 1) * TQ], pO[:]), ss)
                        inc(scalar.activation(ysqb[:], pO[:], AF.Square), ss)

                @blk.vector
                def _(vector, sp=sp, o_ss=o_ss):
                    for ec in range(4):
                        e = sp * 4 + ec
                        vector.wait_ge(ss, o_ss + 2 * ec + 1)
                        inc(vector.tensor_copy(ybf[:], y[:, e * TQ:(e + 1) * TQ]), vv)

                sp_done.append(cur(mm))
                if _os.environ.get("MK_P4SUB", "9") == "0" and sp == 0:
                    barrier(blk)
                    _abort_close()
                    return nc

            if _os.environ.get("MK_P4SUB", "9") < "1":
                barrier(blk)
                _abort_close()
                return nc
            rows4 = pb("p4rows", [1, 3 * TQ], F32)
            emit_ln_rows_bcast(blk, pS4, TQ, mbt, rbt, rows4)
            if _os.environ.get("MK_P4SUB", "9") < "2":
                barrier(blk)
                _abort_close()
                return nc
            barrier(blk)

        xnew = st_p4.enter_context(nc.sbuf_tensor("xnew", [128, NCH * TQ], F32))
        with contextlib.ExitStack() as P:
            pb = lambda *a, **k: P.enter_context(nc.sbuf_tensor(*a, **k))
            xc = [pb(f"p4x{i}", [128, TQ], F32, side="right") for i in range(2)]

            res_vv = cur(vv)
            res_ds = (cur(dsP[0]), cur(dsP[1]))

            @blk.sync
            def _(sync):
                sync.wait_ge(vv, res_vv)
                for c in range(NCH):
                    if c >= 2:
                        sync.wait_ge(vv, res_vv + 3 * (c - 1))
                    inc(sync.dma_start(out=xc[c % 2][:],
                                       in_=xfm[c * 128:(c + 1) * 128, 0:TQ]), dsP[c % 2], 16)

            @blk.vector
            def _(vector):
                for c in range(NCH):
                    vector.wait_ge(dsP[c % 2], res_ds[c % 2] + 16 * (c // 2 + 1))
                    ysl = y[:, c * TQ:(c + 1) * TQ]
                    inc(vector.tensor_tensor(ysl, ysl, mbt[:], ALU.subtract), vv)
                    inc(vector.tensor_tensor(ysl, ysl, rbt[:], ALU.mult), vv)
                    inc(vector.tensor_tensor(xnew[:, c * TQ:(c + 1) * TQ],
                                             xc[c % 2][:], ysl, ALU.add), vv)

            pg_vv = cur(vv)

            @blk.gpsimd
            def _(gpsimd):
                gpsimd.wait_ge(vv, pg_vv)
                for c in range(NCH):
                    inc(gpsimd.dma_start(out=xnewd[c * 128:(c + 1) * 128, :],
                                         in_=xnew[:, c * TQ:(c + 1) * TQ]), dgp, 16)
                if debug:
                    for c in range(NCH):
                        inc(gpsimd.dma_start(out=taps["xnew"][:, c * TQ:(c + 1) * TQ],
                                             in_=xnew[:, c * TQ:(c + 1) * TQ]), dgp, 16)
                        inc(gpsimd.dma_start(out=taps["y"][:, c * TQ:(c + 1) * TQ],
                                             in_=y[:, c * TQ:(c + 1) * TQ]), dgp, 16)

            barrier(blk)

        _close(st_y)
        _close(st_att)

        if upto < 4:
            _abort_close()
            return nc
        st_xq = _mkstack()
        xq = st_xq.enter_context(nc.sbuf_tensor("xq", [128, NCH * TQ], BF16, side="right"))
        st_i1 = _mkstack()
        invs1 = st_i1.enter_context(nc.sbuf_tensor("invs1", [128, TQ], F32, side="right"))
        with contextlib.ExitStack() as P:
            pb = lambda *a, **k: P.enter_context(nc.sbuf_tensor(*a, **k))
            pp = lambda *a, **k: P.enter_context(nc.psum_tensor(*a, **k))
            xb5 = pb("p5xb", [128, TQ], BF16)
            xsq5 = pb("p5xsq", [128, TQ], BF16)
            mbt = pb("p5mb", [128, TQ], F32)
            rbt = pb("p5rb", [128, TQ], F32)
            zt = pb("p5z", [128, TQ], F32)
            si = pb("p5si", [128, TQ], I32)
            tf = pb("p5tf", [128, TQ], F32)
            c2 = pb("p5c2", [128, TQ], F32)
            mpos = pb("p5pos", [128, TQ], F32)
            mneg = pb("p5neg", [128, TQ], F32)
            pS5 = pp("p5ps", [128, 2048], F32)

            rows5 = pb("p5rows", [1, 3 * TQ], F32)
            src = lambda c: xnew[:, c * TQ:(c + 1) * TQ]
            emit_ln_stats(blk, pS5, src, NCH, xb5, xsq5, TQ)
            emit_ln_rows_bcast(blk, pS5, TQ, mbt, rbt, rows5)

            am_vv = cur(vv)

            @blk.vector
            def _(vector):
                vector.wait_ge(vv, am_vv)
                for c in range(NCH):
                    vector.tensor_tensor(zt[:], src(c), mbt[:], ALU.subtract)
                    if c == 0:
                        vector.tensor_copy(mpos[:], zt[:])
                        vector.tensor_copy(mneg[:], zt[:])
                    else:
                        vector.tensor_tensor(mpos[:], mpos[:], zt[:], ALU.max)
                        vector.tensor_tensor(mneg[:], mneg[:], zt[:], ALU.min)
                vector.tensor_scalar(mneg[:], mneg[:], -1.0, None, ALU.mult)
                inc(vector.tensor_tensor(mpos[:], mpos[:], mneg[:], ALU.max), vv)

            cols8 = pb("p5c8", [128, 8], F32)
            rowb = pb("p5rowb", [128, TQ], F32)

            @blk.vector
            def _(vector):
                inc(vector.memset(rowb[:], 0.0), vv)

            emit_pmax_bcast(blk, mpos, mpos, pS5, cols8, rowb)
            am_gg = cur(vv)

            @blk.vector
            def _(vector):
                vector.wait_ge(vv, am_gg)
                vector.tensor_tensor(mpos[:], mpos[:], rbt[:], ALU.mult)
                vector.tensor_scalar(mpos[:], mpos[:], EPS, 1.0 / 128.0, ALU.max, ALU.mult)
                vector.tensor_copy(invs1[:], mpos[:])
                vector.reciprocal(mneg[:], mpos[:])
                vector.tensor_tensor(rbt[:], rbt[:], mneg[:], ALU.mult)
                vector.tensor_tensor(mbt[:], mbt[:], rbt[:], ALU.mult)
                inc(vector.tensor_scalar(mbt[:], mbt[:], -1.0, None, ALU.mult), vv)

            def z_emit(vector, i):
                vector.tensor_tensor(zt[:], src(i), rbt[:], ALU.mult)
                inc(vector.tensor_tensor(zt[:], zt[:], mbt[:], ALU.add), vv)


            def out_emit(vector, i):
                inc(vector.tensor_copy(xq[:, i * TQ:(i + 1) * TQ], tf[:]), vv)

            emit_round_clip_stream(blk, NCH, z_emit, out_emit, zt, si, tf, c2,
                                   -128.0, 127.0, 1)

            if debug:
                dq_vv = cur(vv)

                @blk.gpsimd
                def _(gpsimd):
                    gpsimd.wait_ge(vv, dq_vv)
                    for j in range(4):
                        inc(gpsimd.dma_start(out=taps["xq"][:, j * 4096:(j + 1) * 4096],
                                             in_=xq[:, j * 4096:(j + 1) * 4096]), dgp, 16)

            barrier(blk)

        _close(st_p4)

        if upto < 5:
            _abort_close()
            return nc
        with contextlib.ExitStack() as P:
            pb = lambda *a, **k: P.enter_context(nc.sbuf_tensor(*a, **k))
            wt6 = [pb(f"p6aw{i}", [128, 2048], F32) for i in range(2)]
            red = [pb(f"p6ared{i}", [128, 1], F32) for i in range(2)]
            acc = pb("p6aacc", [128, 4], F32)

            def wsl(wi, idx):
                w = (WIN["wg"], WIN["wu"], WIN["wd"])[wi]
                if wi < 2:
                    return w[idx * 128:(idx + 1) * 128, :]
                r, c = divmod(idx, 4)
                return w[r * 128:(r + 1) * 128, c * 2048:(c + 1) * 2048]

            @blk.vector
            def _(vector):
                inc(vector.memset(acc[:], 0.0), vv)

            a_ds = (cur(dsP[0]), cur(dsP[1]))
            a_vv = cur(vv)

            @blk.sync
            def _(sync):
                n = 0
                for wi in range(3):
                    for idx in range(64):
                        if n >= 2:
                            sync.wait_ge(vv, a_vv + 2 * (n - 1))
                        inc(sync.dma_start(out=wt6[n % 2][:], in_=wsl(wi, idx)), dsP[n % 2], 16)
                        n += 1

            def consume(vector, n):
                wi, idx = divmod(n, 64)
                if idx == 0:
                    return vector.tensor_copy(acc[:, wi:wi + 1], red[n % 2][:])
                return vector.tensor_tensor(acc[:, wi:wi + 1], acc[:, wi:wi + 1],
                                            red[n % 2][:], ALU.add)

            @blk.vector
            def _(vector):
                for n in range(192):
                    vector.wait_ge(dsP[n % 2], a_ds[n % 2] + 16 * (n // 2 + 1))
                    inc(vector.tensor_reduce(red[n % 2][:], wt6[n % 2][:], AX.X,
                                             ALU.add, apply_absolute_value=True), vv)
                    inc(consume(vector, n - 1) if n >= 1 else vector.nop(), vv)
                inc(vector.nop(), vv)
                inc(consume(vector, 191), vv)

            ar_vv = cur(vv)
            pA6 = P.enter_context(nc.psum_tensor("p6aps", [128, 512], F32))

            @blk.tensor
            def _(tensor):
                tensor.wait_ge(vv, ar_vv)
                inc(tensor.matmul(pA6[:, 0:4], s_o128f[:], acc[:, 0:4],
                                  start=True, stop=True), mm)

            ar2_mm = cur(mm)

            @blk.vector
            def _(vector):
                vector.wait_ge(mm, ar2_mm)
                vector.tensor_scalar(s_cols[:, 0:3], pA6[:, 0:3], INV_WN, EPS,
                                     ALU.mult, ALU.max)
                vector.nop()
                inc(vector.reciprocal(s_icol[:, 0:3], s_cols[:, 0:3]), vv)

            if debug:
                dsc_vv = cur(vv)

                @blk.gpsimd
                def _(gpsimd):
                    gpsimd.wait_ge(vv, dsc_vv)
                    inc(gpsimd.dma_start(out=taps["scols"][:, 0:3], in_=s_cols[:, 0:3]), dgp, 16)
                    inc(gpsimd.dma_start(out=taps["scols"][:, 3:6], in_=s_icol[:, 0:3]), dgp, 16)

            barrier(blk)

        if upto < 6:
            _abort_close()
            return nc
        with contextlib.ExitStack() as P:
            pb = lambda *a, **k: P.enter_context(nc.sbuf_tensor(*a, **k))
            wt6 = [pb(f"p6bw{i}", [128, 2048], F32) for i in range(2)]
            tt6 = [pb(f"p6bt{i}", [128, 2048], BF16) for i in range(2)]
            zt6 = pb("p6bz", [128, 2048], F32)
            si6 = pb("p6bsi", [128, 2048], I32)
            tf6 = pb("p6btf", [128, 2048], F32)
            c26 = pb("p6bc2", [128, 2048], F32)

            def wsl(wi, idx):
                w = (WIN["wg"], WIN["wu"], WIN["wd"])[wi]
                if wi < 2:
                    return w[idx * 128:(idx + 1) * 128, :]
                r, c = divmod(idx, 4)
                return w[r * 128:(r + 1) * 128, c * 2048:(c + 1) * 2048]

            def tsl(wi, idx):
                t = (tgd, tud, tdd)[wi]
                if wi < 2:
                    return t[idx * 128:(idx + 1) * 128, :]
                r, c = divmod(idx, 4)
                return t[r * 128:(r + 1) * 128, c * 2048:(c + 1) * 2048]

            b_ds = (cur(dsP[0]), cur(dsP[1]))
            b_gp = (cur(dgP[0]), cur(dgP[1]))
            tern_vv = cur(vv)

            @blk.sync
            def _(sync):
                for n in range(192):
                    if n >= 2:
                        sync.wait_ge(vv, tern_vv + 3 * (n - 1))
                    inc(sync.dma_start(out=wt6[n % 2][:], in_=wsl(n // 64, n % 64)), dsP[n % 2], 16)

            def z_emit6(vector, n):
                vector.wait_ge(dsP[n % 2], b_ds[n % 2] + 16 * (n // 2 + 1))
                inc(vector.tensor_scalar(zt6[:], wt6[n % 2][:],
                                         s_icol[:, n // 64:1 + n // 64], 1.0,
                                         ALU.mult, ALU.min), vv)
                inc(vector.tensor_scalar(zt6[:], zt6[:], -1.0, None, ALU.max), vv)

            def out_emit6(vector, n):
                if n >= 2:
                    vector.wait_ge(dgP[n % 2], b_gp[n % 2] + 16 * (n // 2))
                inc(vector.tensor_copy(tt6[n % 2][:], tf6[:]), vv)

            emit_round_clip_stream(blk, 192, z_emit6, out_emit6, zt6, si6, tf6,
                                   c26, -1.0, 1.0, 2)

            @blk.gpsimd
            def _(gpsimd):
                for n in range(192):
                    gpsimd.wait_ge(vv, tern_vv + 3 * (n + 1))
                    inc(gpsimd.dma_start(out=tsl(n // 64, n % 64), in_=tt6[n % 2][:]), dgP[n % 2], 16)

            tern_done = (cur(dgP[0]), cur(dgP[1]))
            if debug:
                @blk.gpsimd
                def _(gpsimd):
                    gpsimd.wait_ge(dgP[0], tern_done[0])
                    gpsimd.wait_ge(dgP[1], tern_done[1])
                    inc(gpsimd.dma_start(out=taps["tgd"][:], in_=tgd[0:1024, :]), dgp, 16)
                    inc(gpsimd.dma_start(out=taps["tud"][:], in_=tud[0:1024, :]), dgp, 16)
            barrier(blk)

        if upto < 7:
            _abort_close()
            return nc
        with contextlib.ExitStack() as P:
            pb = lambda *a, **k: P.enter_context(nc.sbuf_tensor(*a, **k))
            pp = lambda *a, **k: P.enter_context(nc.psum_tensor(*a, **k))
            wbg = pb("p6cwg", [128, NCH * 512], BF16)
            wbu = pb("p6cwu", [128, NCH * 512], BF16)
            bgb = pb("p6cbg", [128, TQ], F32)
            bub = pb("p6cbu", [128, TQ], F32)
            t1_ = pb("p6ct1", [128, TQ], F32)
            t2_ = pb("p6ct2", [128, TQ], F32)
            inter = pb("p6ci", [128, TQ], BF16)
            sae = [pb(f"p6cs{i}", [128, TQ], BF16) for i in range(2)]
            pG_ = pp("p6cpg", [128, 1024], F32)
            pU_ = pp("p6cpu", [128, 1024], F32)
            pA_ = pp("p6cpa", [128, 1024], F32)

            c_vv0 = cur(vv)

            @blk.vector
            def _(vector):
                vector.wait_ge(vv, c_vv0)
                vector.tensor_scalar(bgb[:], invs1[:], s_cols[:, 0:1], None, ALU.mult)
                inc(vector.tensor_scalar(bub[:], invs1[:], s_cols[:, 1:2], None, ALU.mult), vv)

            gd_gp0 = (cur(dgP[0]), cur(dgP[1]))
            for osp in range(16):
                l_dsy = cur(dsy)

                @blk.sync
                def _(sync, osp=osp):
                    if osp == 0:
                        sync.wait_ge(dgP[0], tern_done[0])
                        sync.wait_ge(dgP[1], tern_done[1])
                    else:
                        sync.wait_ge(mm, span_mm[osp - 1])
                    for dc in range(NCH):
                        inc(sync.dma_start(out=wbg[:, dc * 512:(dc + 1) * 512],
                                           in_=tgd[osp * 512:(osp + 1) * 512,
                                                   dc * 128:(dc + 1) * 128],
                                           transpose=True), dsy, 16)
                    for dc in range(NCH):
                        inc(sync.dma_start(out=wbu[:, dc * 512:(dc + 1) * 512],
                                           in_=tud[osp * 512:(osp + 1) * 512,
                                                   dc * 128:(dc + 1) * 128],
                                           transpose=True), dsy, 16)

                o_mm, o_vv, o_ss = cur(mm), cur(vv), cur(ss)

                @blk.tensor
                def _(tensor, osp=osp, l_dsy=l_dsy, o_mm=o_mm, o_vv=o_vv):
                    tensor.wait_ge(dsy, l_dsy + 16 * 32)
                    for oc in range(4):
                        if oc >= 1:
                            tensor.wait_ge(vv, o_vv + 3 * oc - 1)
                        for dc in range(NCH):
                            for j in range(2):
                                tensor.matmul(pG_[:, j * 512:(j + 1) * 512],
                                              wbg[:, dc * 512 + oc * 128:dc * 512 + oc * 128 + 128],
                                              xq[:, dc * TQ + j * 512:dc * TQ + (j + 1) * 512],
                                              start=(dc == 0), stop=(dc == NCH - 1))
                        for dc in range(NCH):
                            for j in range(2):
                                r = tensor.matmul(pU_[:, j * 512:(j + 1) * 512],
                                                  wbu[:, dc * 512 + oc * 128:dc * 512 + oc * 128 + 128],
                                                  xq[:, dc * TQ + j * 512:dc * TQ + (j + 1) * 512],
                                                  start=(dc == 0), stop=(dc == NCH - 1))
                        inc(r, mm)
                        tensor.wait_ge(vv, o_vv + 3 * oc + 2)
                        r = None
                        for j in range(2):
                            r = tensor.matmul(pA_[:, j * 512:(j + 1) * 512], s_h128[:],
                                              inter[:, j * 512:(j + 1) * 512],
                                              start=True, stop=True)
                        inc(r, mm)

                @blk.scalar
                def _(scalar, o_vv=o_vv):
                    for oc in range(4):
                        scalar.wait_ge(vv, o_vv + 3 * oc + 1)
                        inc(scalar.activation(t2_[:], t1_[:], AF.Silu), ss)

                @blk.vector
                def _(vector, osp=osp, o_mm=o_mm, o_ss=o_ss, o_vv=o_vv):
                    for oc in range(4):
                        cg = osp * 4 + oc
                        vector.wait_ge(mm, o_mm + 2 * oc + 1)
                        inc(vector.tensor_tensor(t1_[:], pG_[:], bgb[:], ALU.mult), vv)
                        vector.wait_ge(ss, o_ss + oc + 1)
                        vector.tensor_tensor(t1_[:], t2_[:], bub[:], ALU.mult)
                        inc(vector.tensor_tensor(inter[:], t1_[:], pU_[:], ALU.mult), vv)
                        vector.wait_ge(mm, o_mm + 2 * oc + 2)
                        if cg >= 2:
                            vector.wait_ge(dgP[cg % 2], gd_gp0[cg % 2] + 32 * (cg // 2))
                        inc(vector.tensor_copy(sae[cg % 2][:], pA_[:]), vv)

                @blk.gpsimd
                def _(gpsimd, osp=osp, o_vv=o_vv):
                    for oc in range(4):
                        cg = osp * 4 + oc
                        gpsimd.wait_ge(vv, o_vv + 3 * (oc + 1))
                        inc(gpsimd.dma_start(out=fwbd[cg * 64:(cg + 1) * 64, :],
                                             in_=sae[cg % 2][0:64, :]), dgP[cg % 2], 16)
                        inc(gpsimd.dma_start(out=fwbd[4096 + cg * 64:4096 + (cg + 1) * 64, :],
                                             in_=sae[cg % 2][64:128, :]), dgP[cg % 2], 16)

                span_mm = span_mm if osp else []
                span_mm.append(cur(mm))

            fwb_done = (cur(dgP[0]), cur(dgP[1]))
            if debug:
                @blk.gpsimd
                def _(gpsimd):
                    gpsimd.wait_ge(dgP[0], fwb_done[0])
                    gpsimd.wait_ge(dgP[1], fwb_done[1])
                    inc(gpsimd.dma_start(out=taps["fwba"][:], in_=fwbd[0:2048, :]), dgp, 16)
            barrier(blk)

        _close(st_i1)
        _close(st_xq)

        if upto < 8:
            _abort_close()
            return nc
        st_fw = _mkstack()
        fw = st_fw.enter_context(nc.sbuf_tensor("fw", [128, 64 * TQ], BF16))
        st_bd = _mkstack()
        bdb = st_bd.enter_context(nc.sbuf_tensor("bdb", [128, TQ], F32))
        st_a2 = _mkstack()
        a2b = st_a2.enter_context(nc.sbuf_tensor("a2b", [128, TQ], F32))
        fwb4 = fwbd.rearrange("(a c m) t -> a c m t", a=2, c=64, m=64)
        with contextlib.ExitStack() as P:
            pb = lambda *a, **k: P.enter_context(nc.sbuf_tensor(*a, **k))
            pp = lambda *a, **k: P.enter_context(nc.psum_tensor(*a, **k))
            zb = [pb(f"p6dz{i}", [128, TQ], BF16) for i in range(2)]
            fpos = pb("p6dpos", [128, TQ], F32)
            fneg = pb("p6dneg", [128, TQ], F32)
            pB_ = pp("p6dpb", [128, 2048], F32)

            d_ds = (cur(dsP[0]), cur(dsP[1]))
            d_mm = cur(mm)

            @blk.sync
            def _(sync):
                sync.wait_ge(dgP[0], fwb_done[0])
                sync.wait_ge(dgP[1], fwb_done[1])
                for m in range(64):
                    if m >= 2:
                        sync.wait_ge(mm, d_mm + (m - 1))
                    inc(sync.dma_start(out=zb[m % 2][:],
                                       in_=fwb4[:, :, m, :].rearrange("a c t -> (a c) t")),
                        dsP[m % 2], 16)

            d_ss, d_vv = cur(ss), cur(vv)

            @blk.tensor
            def _(tensor):
                for m in range(64):
                    tensor.wait_ge(dsP[m % 2], d_ds[m % 2] + 16 * (m // 2 + 1))
                    if m >= 2:
                        tensor.wait_ge(ss, d_ss + (m - 1))
                        tensor.wait_ge(vv, d_vv + (m - 1))
                    slot = (m % 2) * 1024
                    r = None
                    for j in range(2):
                        r = tensor.matmul(pB_[:, slot + j * 512:slot + (j + 1) * 512],
                                          s_h64b[:], zb[m % 2][:, j * 512:(j + 1) * 512],
                                          start=True, stop=True)
                    inc(r, mm)

            @blk.scalar
            def _(scalar):
                for m in range(64):
                    scalar.wait_ge(mm, d_mm + m + 1)
                    slot = (m % 2) * 1024
                    inc(scalar.copy(fw[:, m * TQ:(m + 1) * TQ],
                                    pB_[:, slot:slot + 1024]), ss)

            @blk.vector
            def _(vector):
                for m in range(64):
                    vector.wait_ge(ss, d_ss + m + 1)
                    fsl = fw[:, m * TQ:(m + 1) * TQ]
                    if m == 0:
                        vector.tensor_copy(fpos[:], fsl)
                        inc(vector.tensor_copy(fneg[:], fsl), vv)
                    else:
                        vector.tensor_tensor(fpos[:], fpos[:], fsl, ALU.max)
                        inc(vector.tensor_tensor(fneg[:], fneg[:], fsl, ALU.min), vv)

            f_vv = cur(vv)

            @blk.vector
            def _(vector):
                vector.wait_ge(vv, f_vv)
                vector.tensor_scalar(fneg[:], fneg[:], -1.0, None, ALU.mult)
                inc(vector.tensor_tensor(fpos[:], fpos[:], fneg[:], ALU.max), vv)

            cols8d = pb("p6dc8", [128, 8], F32)
            rowbd = pb("p6drowb", [128, TQ], F32)

            @blk.vector
            def _(vector):
                inc(vector.memset(rowbd[:], 0.0), vv)

            emit_pmax_bcast(blk, fpos, fpos, pB_, cols8d, rowbd)
            f_gg = cur(vv)

            @blk.vector
            def _(vector):
                vector.wait_ge(vv, f_gg)
                vector.tensor_scalar(fpos[:], fpos[:], C_FWHT, EPS, ALU.mult, ALU.max)
                vector.tensor_scalar(bdb[:], fpos[:], s_cols[:, 2:3], 1.0 / 128.0,
                                     ALU.mult, ALU.mult)
                vector.reciprocal(fneg[:], fpos[:])
                inc(vector.tensor_scalar(a2b[:], fneg[:], 128.0 * C_FWHT, None, ALU.mult), vv)

            if debug:
                dpre_vv = cur(vv)

                @blk.gpsimd
                def _(gpsimd):
                    gpsimd.wait_ge(vv, dpre_vv)
                    for j in range(4):
                        inc(gpsimd.dma_start(out=taps["fwpre"][:, j * 4096:(j + 1) * 4096],
                                             in_=fw[:, j * 4096:(j + 1) * 4096]), dgp, 16)
                    inc(gpsimd.dma_start(out=taps["bdb"][:], in_=bdb[:]), dgp, 16)
                    inc(gpsimd.dma_start(out=taps["a2b"][:], in_=a2b[:]), dgp, 16)
                    gpsimd.wait_ge(dgp, cur(dgp))

            barrier(blk)

        if upto < 9:
            _abort_close()
            return nc
        with contextlib.ExitStack() as P:
            pb = lambda *a, **k: P.enter_context(nc.sbuf_tensor(*a, **k))
            zt7 = pb("p6ez", [128, TQ], F32)
            si7 = pb("p6esi", [128, TQ], I32)
            tf7 = pb("p6etf", [128, TQ], F32)
            c27 = pb("p6ec2", [128, TQ], F32)

            def z_emit7(vector, m):
                inc(vector.tensor_tensor(zt7[:], fw[:, m * TQ:(m + 1) * TQ],
                                         a2b[:], ALU.mult), vv)

            def out_emit7(vector, m):
                inc(vector.tensor_copy(fw[:, m * TQ:(m + 1) * TQ], tf7[:]), vv)

            emit_round_clip_stream(blk, 64, z_emit7, out_emit7, zt7, si7, tf7,
                                   c27, -128.0, 127.0, 1)

            if debug:
                dfw_vv = cur(vv)

                @blk.gpsimd
                def _(gpsimd):
                    gpsimd.wait_ge(vv, dfw_vv)
                    for j in range(16):
                        inc(gpsimd.dma_start(out=taps["fw"][:, j * 4096:(j + 1) * 4096],
                                             in_=fw[:, j * 4096:(j + 1) * 4096]), dgp, 16)

            barrier(blk)

        _close(st_a2)

        if upto < 10:
            _abort_close()
            return nc
        st_ln4 = _mkstack()
        ysum = st_ln4.enter_context(nc.sbuf_tensor("ysum", [128, TQ], F32, side="right"))
        yssq = st_ln4.enter_context(nc.sbuf_tensor("yssq", [128, TQ], F32, side="right"))
        with contextlib.ExitStack() as P:
            pb = lambda *a, **k: P.enter_context(nc.sbuf_tensor(*a, **k))
            pp = lambda *a, **k: P.enter_context(nc.psum_tensor(*a, **k))
            wbd = pb("p6fwd", [128, 32 * 512], BF16)
            y2c = [pb(f"p6fy{i}", [128, TQ], F32, side="right") for i in range(2)]
            y2sq = pb("p6fysq", [128, TQ], F32, side="right")
            pD_ = pp("p6fpd", [128, 4096], F32)

            for hs in range(4):
                halves_mm = []
                for half in range(2):
                    dn_dsy = cur(dsy)

                    @blk.sync
                    def _(sync, hs=hs, half=half):
                        if hs + half > 0:
                            sync.wait_ge(mm, last_dn_mm[0])
                        for mi in range(32):
                            m = half * 32 + mi
                            inc(sync.dma_start(out=wbd[:, mi * 512:(mi + 1) * 512],
                                               in_=tdd[hs * 512:(hs + 1) * 512,
                                                       m * 128:(m + 1) * 128],
                                               transpose=True), dsy, 16)

                    @blk.tensor
                    def _(tensor, hs=hs, half=half, dn_dsy=dn_dsy):
                        tensor.wait_ge(dsy, dn_dsy + 16 * 32)
                        if hs > 0 and half == 0:
                            tensor.wait_ge(vv, ev_done[0])
                        for hc in range(4):
                            for mi in range(32):
                                m = half * 32 + mi
                                for j in range(2):
                                    r = tensor.matmul(
                                        pD_[:, hc * 1024 + j * 512:hc * 1024 + (j + 1) * 512],
                                        wbd[:, mi * 512 + hc * 128:mi * 512 + hc * 128 + 128],
                                        fw[:, m * TQ + j * 512:m * TQ + (j + 1) * 512],
                                        start=(half == 0 and mi == 0),
                                        stop=(half == 1 and mi == 31))
                        inc(r, mm)

                    last_dn_mm = [cur(mm)]
                    halves_mm.append(cur(mm))

                ev_mm, ev_vv, ev_ss = cur(mm), cur(vv), cur(ss)

                ev_gp = cur(dgp)

                @blk.vector
                def _(vector, hs=hs, ev_mm=ev_mm, ev_ss=ev_ss, ev_gp=ev_gp):
                    vector.wait_ge(mm, ev_mm)
                    vector.wait_ge(dgp, ev_gp)
                    for hc in range(4):
                        h = hs * 4 + hc
                        inc(vector.tensor_tensor(y2c[hc % 2][:],
                                                 pD_[:, hc * 1024:(hc + 1) * 1024],
                                                 bdb[:], ALU.mult), vv)
                        if h == 0:
                            vector.tensor_copy(ysum[:], y2c[hc % 2][:])
                        else:
                            vector.tensor_tensor(ysum[:], ysum[:], y2c[hc % 2][:], ALU.add)
                        vector.wait_ge(ss, ev_ss + hc + 1)
                        if h == 0:
                            inc(vector.tensor_copy(yssq[:], y2sq[:]), vv)
                        else:
                            inc(vector.tensor_tensor(yssq[:], yssq[:], y2sq[:], ALU.add), vv)

                @blk.scalar
                def _(scalar, ev_vv=ev_vv):
                    for hc in range(4):
                        scalar.wait_ge(vv, ev_vv + 2 * hc + 1)
                        inc(scalar.activation(y2sq[:], y2c[hc % 2][:], AF.Square), ss)

                @blk.gpsimd
                def _(gpsimd, hs=hs, ev_vv=ev_vv):
                    for hc in range(4):
                        h = hs * 4 + hc
                        gpsimd.wait_ge(vv, ev_vv + 2 * hc + 1)
                        inc(gpsimd.dma_start(out=y2dd[h * 128:(h + 1) * 128, :],
                                             in_=y2c[hc % 2][:]), dgp, 16)

                ev_done = [cur(vv)]

            y2_done = cur(dgp)
            barrier(blk)

        _close(st_bd)
        _close(st_fw)

        if upto < 11:
            _abort_close()
            return nc
        with contextlib.ExitStack() as P:
            pb = lambda *a, **k: P.enter_context(nc.sbuf_tensor(*a, **k))
            m4 = pb("p7m4", [128, TQ], F32)
            r4 = pb("p7r4", [128, TQ], F32)
            yc7 = [pb(f"p7y{i}", [128, TQ], F32) for i in range(2)]
            xc7 = [pb(f"p7x{i}", [128, TQ], F32) for i in range(2)]
            ob7 = [pb(f"p7o{i}", [128, TQ], F32) for i in range(2)]

            s7_vv = cur(vv)
            pS7b = P.enter_context(nc.psum_tensor("p7ps2", [128, 2048], F32))

            @blk.tensor
            def _(tensor):
                tensor.wait_ge(vv, s7_vv)
                for j in range(2):
                    tensor.matmul(pS7b[:, j * 512:(j + 1) * 512], s_o128f[:],
                                  ysum[:, j * 512:(j + 1) * 512], start=True, stop=True)
                    r = tensor.matmul(pS7b[:, 1024 + j * 512:1024 + (j + 1) * 512],
                                      s_o128f[:], yssq[:, j * 512:(j + 1) * 512],
                                      start=True, stop=True)
                inc(r, mm)

            s7c_mm = cur(mm)

            @blk.vector
            def _(vector):
                vector.wait_ge(mm, s7c_mm)
                vector.tensor_scalar(m4[:], pS7b[:, 0:1024], INV_D, None, ALU.mult)
                vector.tensor_scalar(r4[:], pS7b[:, 1024:2048], INV_D, None, ALU.mult)
                vector.tensor_tensor(ysum[:], m4[:], m4[:], ALU.mult)
                vector.tensor_tensor(r4[:], r4[:], ysum[:], ALU.subtract)
                inc(vector.tensor_scalar(r4[:], r4[:], EPS, None, ALU.add), vv)

            s7_vv2 = cur(vv)

            @blk.scalar
            def _(scalar):
                scalar.wait_ge(vv, s7_vv2)
                scalar.activation(r4[:], r4[:], AF.Ln)
                inc(scalar.activation(r4[:], r4[:], AF.Exp, scale=-0.5), ss)

            s7_ss = cur(ss)
            s7_dgp = (cur(dgP[0]), cur(dgP[1]))

            f7_vv = cur(vv)
            f7_ds = (cur(dsP[0]), cur(dsP[1]))

            @blk.sync
            def _(sync):
                sync.wait_ge(ss, s7_ss)
                sync.wait_ge(dgp, y2_done)
                for c in range(NCH):
                    if c >= 2:
                        sync.wait_ge(vv, f7_vv + 3 * (c - 1))
                    inc(sync.dma_start(out=yc7[c % 2][:],
                                       in_=y2dd[c * 128:(c + 1) * 128, :]), dsP[c % 2], 16)
                    inc(sync.dma_start(out=xc7[c % 2][:],
                                       in_=xnewd[c * 128:(c + 1) * 128, :]), dsP[c % 2], 16)

            @blk.vector
            def _(vector):
                for c in range(NCH):
                    vector.wait_ge(dsP[c % 2], f7_ds[c % 2] + 32 * (c // 2 + 1))
                    if c >= 2:
                        vector.wait_ge(dgP[c % 2], s7_dgp[c % 2] + 16 * (c // 2))
                    inc(vector.tensor_tensor(yc7[c % 2][:], yc7[c % 2][:], m4[:], ALU.subtract), vv)
                    inc(vector.tensor_tensor(yc7[c % 2][:], yc7[c % 2][:], r4[:], ALU.mult), vv)
                    inc(vector.tensor_tensor(ob7[c % 2][:], xc7[c % 2][:], yc7[c % 2][:],
                                             ALU.add), vv)

            @blk.gpsimd
            def _(gpsimd):
                for c in range(NCH):
                    gpsimd.wait_ge(vv, f7_vv + 3 * (c + 1))
                    inc(gpsimd.dma_start(out=outfm[c * 128:(c + 1) * 128, :],
                                         in_=ob7[c % 2][:]), dgP[c % 2], 16)
                gpsimd.wait_ge(dgP[0], cur(dgP[0]))
                gpsimd.wait_ge(dgP[1], cur(dgP[1]))
                gpsimd.wait_ge(dgp, cur(dgp))

        _close(st_ln4)

    return nc



def _sylvester(n):
    h = np.array([[1.0]], dtype=np.float64)
    while h.shape[0] < n:
        h = np.block([[h, h], [h, -h]])
    return h


def _host_constants():
    half = 64
    inv_freq = 1.0 / (10000.0 ** (np.arange(half, dtype=np.float32) / half))
    h128 = _sylvester(128).astype(ml_dtypes.bfloat16)
    h64 = _sylvester(64)
    h64b = np.zeros((128, 128), dtype=np.float64)
    h64b[0:64, 0:64] = h64
    h64b[64:128, 64:128] = h64
    h64b = h64b.astype(ml_dtypes.bfloat16)
    ident = np.eye(128, dtype=ml_dtypes.bfloat16)
    return inv_freq, h128, h64b, ident


def _rope_tables(pos, inv_freq):
    ang = pos[None, :].astype(np.float32) * inv_freq[:, None]
    cos = np.cos(ang).astype(np.float32)
    sin = np.sin(ang).astype(np.float32)
    t1 = np.concatenate([cos, cos], axis=0)
    t2 = np.concatenate([-sin, sin], axis=0)
    return np.ascontiguousarray(t1), np.ascontiguousarray(t2)


_ROPE_PERM = None
_WD_PERM = None


def _perms():
    global _ROPE_PERM, _WD_PERM
    if _ROPE_PERM is None:
        p = np.zeros(D, dtype=np.int64)
        for h in range(HN):
            base = h * 128
            p[base:base + 64] = base + np.arange(64) * 2
            p[base + 64:base + 128] = base + np.arange(64) * 2 + 1
        _ROPE_PERM = p
        q = np.zeros(II, dtype=np.int64)
        for m in range(64):
            for pp in range(128):
                lo6, cp = divmod(pp, 64)
                q[m * 128 + pp] = cp * 128 + lo6 * 64 + m
        _WD_PERM = q
    return _ROPE_PERM, _WD_PERM


_NC_CACHE = {}


def kernel(L0_attn_x, L0_attn_Wq, L0_attn_bq, L0_attn_Wk, L0_attn_bk,
           L0_attn_Wv, L0_attn_bv, L0_attn_Wo, L0_attn_bo,
           L0_ffn_w_gate, L0_ffn_w_up, L0_ffn_w_down, _debug=False):
    x = np.asarray(L0_attn_x, dtype=np.float32)
    rp, wdp = _perms()
    inv_freq, h128, h64b, ident = _host_constants()

    wq = np.ascontiguousarray(np.asarray(L0_attn_Wq, np.float32)[rp, :])
    wk = np.ascontiguousarray(np.asarray(L0_attn_Wk, np.float32)[rp, :])
    wv = np.ascontiguousarray(np.asarray(L0_attn_Wv, np.float32))
    wo = np.ascontiguousarray(np.asarray(L0_attn_Wo, np.float32))
    bq = np.ascontiguousarray(np.asarray(L0_attn_bq, np.float32)[rp].reshape(1, D))
    bk = np.ascontiguousarray(np.asarray(L0_attn_bk, np.float32)[rp].reshape(1, D))
    bv = np.ascontiguousarray(np.asarray(L0_attn_bv, np.float32).reshape(1, D))
    bo = np.ascontiguousarray(np.asarray(L0_attn_bo, np.float32).reshape(1, D))
    wg = np.ascontiguousarray(np.asarray(L0_ffn_w_gate, np.float32))
    wu = np.ascontiguousarray(np.asarray(L0_ffn_w_up, np.float32))
    wd = np.ascontiguousarray(np.asarray(L0_ffn_w_down, np.float32)[:, wdp])

    import os
    upto = int(os.environ.get("MK_UPTO", "99"))
    key = ("dbg" if _debug else "plain", upto)
    if key not in _NC_CACHE:
        _NC_CACHE[key] = build_nc(debug=_debug, upto=upto)
    nc = _NC_CACHE[key]

    in_maps = []
    for c in range(8):
        b, half = divmod(c, 2)
        q0 = half * TQ
        tok = np.concatenate([np.arange(q0, q0 + TQ),
                              np.arange((1 - half) * TQ, (1 - half) * TQ + TQ)])
        xfm = np.ascontiguousarray(x[b].T[:, tok])
        pos_q = tok[:TQ]
        pos_k = tok
        t1q, t2q = _rope_tables(pos_q, inv_freq)
        t1k, t2k = _rope_tables(pos_k, inv_freq)
        mask = (pos_k[:, None] <= pos_q[None, :]).astype(ml_dtypes.bfloat16)
        mask = np.ascontiguousarray(mask)
        im = {"xfm": xfm, "wq": wq, "wk": wk, "wv": wv, "wo": wo,
              "bq": bq, "bk": bk, "bv": bv, "bo": bo,
              "wg": wg, "wu": wu, "wd": wd,
              "t1q": t1q, "t2q": t2q, "t1k": t1k, "t2k": t2k,
              "masks": mask, "h128": h128, "h64b": h64b, "ident": ident,
              "identf": np.eye(128, dtype=np.float32)}
        in_maps.append(im)

    res = run_bass_kernel_spmd(nc, in_maps, core_ids=list(range(8)))

    out = np.empty((B, S, D), dtype=np.float32)
    for c in range(8):
        b, half = divmod(c, 2)
        q0 = half * TQ
        ofm = np.asarray(res.results[c]["outfm"])
        out[b, q0:q0 + TQ, :] = ofm.T
    if _debug:
        return out, res
    return out
